# version 02
# baseline (speedup 1.0000x reference)
import numpy as np
import ml_dtypes
from contextlib import ExitStack

import concourse.bass as bass
import concourse.mybir as mybir
from concourse.bass_utils import run_bass_kernel_spmd

F32 = mybir.dt.float32
BF16 = mybir.dt.bfloat16
I32 = mybir.dt.int32
AX = mybir.AxisListType.X
OP = mybir.AluOpType
AF = mybir.ActivationFunctionType

D = 1024
DT = 8
INNER = 4096
IT = 32
TOK = 4096
NT = 32
NCHUNK = 8
MASKVAL = -1920.0
EXPC = -20.0
NSUP = 4
KDEBUG = False
PHASES = 99


class Sem:
    def __init__(self, nc, name):
        self.h = nc.alloc_semaphore(name)
        self.v = 0

    def inc(self, instr, n=1):
        instr.then_inc(self.h, n)
        self.v += n
        return self.v


def build_program():
    nc = bass.Bass("TRN2", target_bir_lowering=False, debug=False,
                   detect_race_conditions=False)

    x_in = nc.dram_tensor("x_loc", [TOK, D], F32, kind="ExternalInput").ap()
    wq_in = nc.dram_tensor("wq", [D, D], F32, kind="ExternalInput").ap()
    wk_in = nc.dram_tensor("wk", [D, D], F32, kind="ExternalInput").ap()
    wv_in = nc.dram_tensor("wv", [D, D], F32, kind="ExternalInput").ap()
    wg_in = nc.dram_tensor("wg", [INNER, D], F32, kind="ExternalInput").ap()
    wu_in = nc.dram_tensor("wu", [INNER, D], F32, kind="ExternalInput").ap()
    wd_in = nc.dram_tensor("wd", [D, INNER], F32, kind="ExternalInput").ap()
    ident_in = nc.dram_tensor("ident_bf", [128, 128], BF16, kind="ExternalInput").ap()
    ones128_in = nc.dram_tensor("ones128", [128, 128], BF16, kind="ExternalInput").ap()
    ones512_in = nc.dram_tensor("ones512", [128, 512], BF16, kind="ExternalInput").ap()
    mask_in = [nc.dram_tensor(f"mask{i}", [128, 128], BF16, kind="ExternalInput").ap()
               for i in range(3)]
    ipos_in = nc.dram_tensor("iposn", [128, 4, 512], BF16, kind="ExternalInput").ap()

    out_d = nc.dram_tensor("out_loc", [TOK, D], F32, kind="ExternalOutput").ap()

    wgu_t = nc.dram_tensor("wgu_t", [DT, 128, 2 * INNER], BF16).ap()
    wd_t = nc.dram_tensor("wd_t", [IT, 128, D], BF16).ap()
    x1_d = nc.dram_tensor("x1", [TOK, D], F32).ap()
    x2_d = nc.dram_tensor("x2", [TOK, D], F32).ap()
    k2_d = nc.dram_tensor("k2sp", [NT, 128, INNER], BF16).ap()
    qt_d = nc.dram_tensor("qt", [2, DT, 128, TOK], BF16).ap()
    cck_in = nc.dram_tensor("cck_in", [2, DT, 128, TOK], BF16).ap()
    cck_out = nc.dram_tensor("cck_out", [2, 2, DT, 128, TOK], BF16).ap()
    ccv_in = nc.dram_tensor("ccv_in", [2, NT, 128, D], BF16).ap()
    ccv_out = nc.dram_tensor("ccv_out", [2, 2, NT, 128, D], BF16).ap()

    es = ExitStack()
    ident = es.enter_context(nc.sbuf_tensor("c_ident", [128, 128], BF16))
    ones128 = es.enter_context(nc.sbuf_tensor("c_ones128", [128, 128], BF16))
    ones512 = es.enter_context(nc.sbuf_tensor("c_ones512", [128, 512], BF16))
    masks = es.enter_context(nc.sbuf_tensor("c_masks", [128, 3, 128], BF16))
    MU0, MU1, MF = range(3)
    iposn = es.enter_context(nc.sbuf_tensor("c_ipos", [128, 4, 512], BF16))
    wdeq = es.enter_context(nc.sbuf_tensor("c_wdeq", [128, 3], F32))
    s2r_save = es.enter_context(nc.sbuf_tensor("c_s2r", [128, NT], F32))
    smt = es.enter_context(nc.sbuf_tensor("c_smt", [128, 24], F32))

    sCN = Sem(nc, "sCN")
    sLP = [Sem(nc, "sL0"), Sem(nc, "sL1")]
    sSP = [Sem(nc, "sS0"), Sem(nc, "sS1")]
    sGP = [Sem(nc, "sG0"), Sem(nc, "sG1")]
    sBK = Sem(nc, "sBK")
    sGD = Sem(nc, "sGD")
    sST = Sem(nc, "sST")
    sPE = Sem(nc, "sPE")
    sDV = Sem(nc, "sDV")
    sAC = Sem(nc, "sAC")
    sCC = Sem(nc, "sCC")
    all_sems = [sCN, sBK, sGD, sST, sPE, sDV, sAC, sCC] + sLP + sSP + sGP

    sy, pe, dv, ac, gp = nc.sync, nc.tensor, nc.vector, nc.scalar, nc.gpsimd

    nc.all_engine_barrier()
    for s_ in all_sems:
        gp.sem_clear(s_.h)
    nc.all_engine_barrier()

    sCN.inc(sy.dma_start(out=ident[:], in_=ident_in), 16)
    sCN.inc(sy.dma_start(out=ones128[:], in_=ones128_in), 16)
    sCN.inc(sy.dma_start(out=ones512[:], in_=ones512_in), 16)
    for i in range(3):
        sCN.inc(sy.dma_start(out=masks[:, i, :], in_=mask_in[i]), 16)
    sCN.inc(sy.dma_start(out=iposn[:], in_=ipos_in), 16)
    v_consts = sCN.v
    dv.wait_ge(sCN.h, v_consts)
    pe.wait_ge(sCN.h, v_consts)

    def rf(instr):
        v = sDV.inc(instr)
        dv.drain()
        return v

    class TPool:
        def __init__(self, tensors):
            self.t = tensors
            self.guard = [0] * len(tensors)
            self.i = 0

        def run(self, dst_sbuf, src_sbuf):
            j = self.i % len(self.t)
            self.i += 1
            if self.guard[j]:
                pe.wait_ge(sDV.h, self.guard[j])
            pv = sPE.inc(pe.matmul(self.t[j][:], src_sbuf, ident[:],
                                   start=True, stop=True, is_transpose=True))
            dv.wait_ge(sPE.h, pv)
            self.guard[j] = sDV.inc(dv.tensor_copy(out=dst_sbuf, in_=self.t[j][:]))
            return self.guard[j]

    def quant_weight(w_ap, n_rows, row_f, deq_col, spill_dst, tag):
        ntile = n_rows // 128
        with (
            nc.sbuf_tensor(f"{tag}_wt", [128, 2, row_f], F32) as wt,
            nc.sbuf_tensor(f"{tag}_acc", [128, 1], F32) as acc,
            nc.sbuf_tensor(f"{tag}_a2", [128, 2], BF16) as a2,
            nc.sbuf_tensor(f"{tag}_i32", [128, row_f], I32) as q32,
            nc.sbuf_tensor(f"{tag}_bf", [128, row_f], BF16) as qbf,
            nc.sbuf_tensor(f"{tag}_tmp", [128, row_f], F32) as qtmp,
            nc.psum_tensor(f"{tag}_ps", [128, 512], F32) as ps,
            nc.psum_tensor(f"{tag}_p0", [128, 128], BF16) as tp0,
            nc.psum_tensor(f"{tag}_p1", [128, 128], BF16) as tp1,
            nc.psum_tensor(f"{tag}_p2", [128, 128], BF16) as tp2,
            nc.psum_tensor(f"{tag}_p3", [128, 128], BF16) as tp3,
        ):
            pool = TPool([tp0, tp1, tp2, tp3])
            dv_mark = 0
            for t in range(ntile):
                if t >= 2:
                    sy.wait_ge(sDV.h, dv_mark)
                v = sLP[t % 2].inc(sy.dma_start(
                    out=wt[:, t % 2, :],
                    in_=w_ap[t * 128:(t + 1) * 128, :]), 16)
                dv.wait_ge(sLP[t % 2].h, v)
                if t == 0:
                    dv_mark = rf(dv.tensor_reduce(
                        acc[:], wt[:, t % 2, :], AX, OP.add,
                        apply_absolute_value=True))
                else:
                    rf(dv.tensor_reduce(smt[:, 0:1], wt[:, t % 2, :], AX, OP.add,
                                        apply_absolute_value=True))
                    dv_mark = rf(dv.tensor_add(acc[:], acc[:], smt[:, 0:1]))
            rf(dv.tensor_copy(out=a2[:, 0:1], in_=acc[:]))
            dv_mark = rf(dv.scalar_tensor_tensor(
                a2[:, 1:2], acc[:], 1.0, a2[:, 0:1],
                op0=OP.mult, op1=OP.subtract))
            pe.wait_ge(sDV.h, dv_mark)
            pdone = sPE.inc(pe.matmul(ps[:, 0:2], ones128[:], a2[:],
                                      start=True, stop=True))
            dv.wait_ge(sPE.h, pdone)
            rf(dv.tensor_copy(out=smt[:, 1:2], in_=ps[:, 0:1]))
            rf(dv.tensor_add(smt[:, 1:2], smt[:, 1:2], ps[:, 1:2]))
            rf(dv.tensor_scalar(smt[:, 1:2], smt[:, 1:2], 1.0 / (n_rows * row_f),
                                1e-5, op0=OP.mult, op1=OP.max))
            dv.tensor_copy(out=wdeq[:, deq_col:deq_col + 1], in_=smt[:, 1:2])
            swv = rf(dv.reciprocal(smt[:, 2:3], smt[:, 1:2]))
            dv_mark = swv
            sy.wait_ge(sDV.h, swv)
            for t in range(ntile):
                if t >= 2:
                    sy.wait_ge(sDV.h, dv_mark)
                v = sLP[t % 2].inc(sy.dma_start(
                    out=wt[:, t % 2, :],
                    in_=w_ap[t * 128:(t + 1) * 128, :]), 16)
                dv.wait_ge(sLP[t % 2].h, v)
                dv.wait_ge(sDV.h, swv)
                dv.tensor_scalar(qtmp[:], wt[:, t % 2, :], smt[:, 2:3], 1.4999999,
                                 op0=OP.mult, op1=OP.min)
                dv.tensor_scalar(q32[:], qtmp[:], -1.4999999, None, op0=OP.max)
                vq = sDV.inc(dv.tensor_copy(out=qbf[:], in_=q32[:]))
                dv_mark = vq
                pe.wait_ge(sDV.h, vq)
                for blk in range(row_f // 128):
                    dv_mark = pool.run(spill_dst(t, blk),
                                       qbf[:, blk * 128:(blk + 1) * 128])

    wd_stage_ctx = nc.sbuf_tensor("st_wd", [128, IT, D], BF16)
    wd_stage = wd_stage_ctx.__enter__()
    quant_weight(wd_in, D, INNER, 2,
                 lambda t, blk: wd_stage[:, blk, t * 128:(t + 1) * 128], "qd")
    sy.wait_ge(sDV.h, sDV.v)
    for it in range(IT):
        sST.inc(sy.dma_start(out=wd_t[it], in_=wd_stage[:, it, :]), 16)
    sy.wait_ge(sST.h, sST.v)
    nc.all_engine_barrier()
    wd_stage_ctx.__exit__(None, None, None)
    if PHASES <= 1:
        es.close()
        return nc

    wgu_ctx = [nc.sbuf_tensor("st_wgu", [128, DT, 2 * INNER], BF16)]
    wgu_holder = [wgu_ctx[0].__enter__()]

    quant_weight(wg_in, INNER, D, 0,
                 lambda t, blk: wgu_holder[0][:, blk, t * 128:(t + 1) * 128], "qg")
    quant_weight(wu_in, INNER, D, 1,
                 lambda t, blk: wgu_holder[0][:, blk, INNER + t * 128:
                                              INNER + (t + 1) * 128], "qu")
    sy.wait_ge(sDV.h, sDV.v)
    for dt in range(DT):
        sST.inc(sy.dma_start(out=wgu_t[dt], in_=wgu_holder[0][:, dt, :]), 16)
    sy.wait_ge(sST.h, sST.v)
    nc.all_engine_barrier()
    if PHASES <= 2:
        wgu_ctx[0].__exit__(None, None, None)
        es.close()
        return nc

    def emit_ln(x_sb, ln_out, sq_out, ld_sem=None, ld_val=0, sq_guard=0):
        rf(dv.tensor_reduce(smt[:, 4:5], x_sb, AX, OP.add))
        if ld_sem is not None:
            ac.wait_ge(ld_sem, ld_val)
        if sq_guard:
            ac.wait_ge(sDV.h, sq_guard)
        a1 = sAC.inc(ac.activation(out=sq_out, in_=x_sb, func=AF.Square,
                                   accum_out=smt[:, 5:6]))
        dv.wait_ge(sAC.h, a1)
        dv.tensor_scalar(smt[:, 6:7], smt[:, 4:5], 1.0 / D, None, op0=OP.mult)
        dv.drain()
        dv.tensor_scalar(smt[:, 7:8], smt[:, 5:6], 1.0 / D, None, op0=OP.mult)
        dv.drain()
        dv.tensor_mul(smt[:, 8:9], smt[:, 6:7], smt[:, 6:7])
        dv.drain()
        dv.tensor_sub(smt[:, 9:10], smt[:, 7:8], smt[:, 8:9])
        dv.drain()
        d1 = sDV.inc(dv.tensor_scalar(smt[:, 9:10], smt[:, 9:10], 1e-5, None,
                                      op0=OP.add))
        dv.drain()
        ac.wait_ge(sDV.h, d1)
        a2 = sAC.inc(ac.activation(out=smt[:, 10:11], in_=smt[:, 9:10],
                                   func=AF.Sqrt))
        dv.wait_ge(sAC.h, a2)
        rf(dv.reciprocal(smt[:, 11:12], smt[:, 10:11]))
        dv.tensor_mul(smt[:, 12:13], smt[:, 11:12], smt[:, 11:12])
        dv.drain()
        dv.tensor_mul(smt[:, 12:13], smt[:, 12:13], smt[:, 9:10])
        dv.drain()
        dv.tensor_scalar(smt[:, 12:13], smt[:, 12:13], -0.5, 1.5,
                         op0=OP.mult, op1=OP.add)
        dv.drain()
        dv.tensor_mul(smt[:, 11:12], smt[:, 11:12], smt[:, 12:13])
        dv.drain()
        dv.tensor_scalar(ln_out, x_sb, smt[:, 6:7], smt[:, 11:12],
                         op0=OP.subtract, op1=OP.mult)

    def emit_ffn_a(x_src, tag):
        wgu = wgu_holder[0]
        with (
            nc.sbuf_tensor(f"{tag}_x", [128, 2, D], F32) as xt,
            nc.sbuf_tensor(f"{tag}_ln", [128, D], F32) as lnb,
            nc.sbuf_tensor(f"{tag}_kqt", [128, DT, 128], BF16) as kqt,
            nc.sbuf_tensor(f"{tag}_gv", [128, 2, 512], F32) as gvb,
            nc.sbuf_tensor(f"{tag}_gs", [128, 2, 512], F32) as gs,
            nc.sbuf_tensor(f"{tag}_h", [128, INNER], F32) as hb,
            nc.sbuf_tensor(f"{tag}_k2i", [128, INNER], I32) as k2i,
            nc.sbuf_tensor(f"{tag}_k2b", [128, INNER], BF16) as k2b,
            nc.sbuf_tensor(f"{tag}_k2t", [128, INNER], BF16) as k2t,
            nc.psum_tensor(f"{tag}_pc0", [128, 512], F32) as pc0,
            nc.psum_tensor(f"{tag}_pc1", [128, 512], F32) as pc1,
            nc.psum_tensor(f"{tag}_t0", [128, 128], BF16) as tp0,
            nc.psum_tensor(f"{tag}_t1", [128, 128], BF16) as tp1,
            nc.psum_tensor(f"{tag}_t2", [128, 128], BF16) as tp2,
            nc.psum_tensor(f"{tag}_t3", [128, 128], BF16) as tp3,
        ):
            pool = TPool([tp0, tp1, tp2, tp3])
            pcs = [pc0, pc1]
            sq_view = k2i[:, 0:D].bitcast(F32)
            kqb = k2b[:, 0:D]
            kqi = k2i[:, 0:D]
            v_x = [0] * (NT + 1)
            x_free_dv = [0, 0]
            x_free_ac = [0, 0]
            pe_tile_mark = 0
            gs_free = [0, 0]
            gv_free = [0, 0]
            bank_free = [0, 0]
            v_spill = [0] * NT
            vk2_prev = 0
            v_x[0] = sLP[0].inc(sy.dma_start(out=xt[:, 0, :], in_=x_src[0:128, :]), 16)
            for t in range(NT):
                if t + 1 < NT:
                    sy.wait_ge(sDV.h, x_free_dv[(t + 1) % 2])
                    sy.wait_ge(sAC.h, x_free_ac[(t + 1) % 2])
                    v_x[t + 1] = sLP[(t + 1) % 2].inc(sy.dma_start(
                        out=xt[:, (t + 1) % 2, :],
                        in_=x_src[(t + 1) * 128:(t + 2) * 128, :]), 16)
                xs = xt[:, t % 2, :]
                dv.wait_ge(sPE.h, pe_tile_mark)
                dv.wait_ge(sLP[t % 2].h, v_x[t])
                emit_ln(xs, lnb[:], sq_view, sLP[t % 2].h, v_x[t],
                        sq_guard=vk2_prev)
                x_free_ac[t % 2] = sAC.v
                rf(dv.tensor_reduce(smt[:, 13:14], lnb[:], AX, OP.max,
                                    apply_absolute_value=True))
                rf(dv.tensor_scalar(smt[:, 13:14], smt[:, 13:14], 1e-5, None,
                                    op0=OP.max))
                rf(dv.reciprocal(smt[:, 14:15], smt[:, 13:14]))
                rf(dv.tensor_scalar(smt[:, 14:15], smt[:, 14:15], 127.0, None,
                                    op0=OP.mult))
                rf(dv.tensor_scalar(smt[:, 13:14], smt[:, 13:14], 1.0 / 127, None,
                                    op0=OP.mult))
                rg = smt[:, 0 + t % 2:1 + t % 2]
                ru = smt[:, 2 + t % 2:3 + t % 2]
                rf(dv.tensor_mul(rg, smt[:, 13:14], wdeq[:, 0:1]))
                rf(dv.tensor_mul(ru, smt[:, 13:14], wdeq[:, 1:2]))
                dv.tensor_scalar(kqi, lnb[:], smt[:, 14:15], None, op0=OP.mult)
                vq = sDV.inc(dv.tensor_copy(out=kqb, in_=kqi))
                x_free_dv[t % 2] = vq
                pe.wait_ge(sDV.h, vq)
                for dt in range(DT):
                    pool.run(kqt[:, dt, :], kqb[:, dt * 128:(dt + 1) * 128])
                vkqt = sDV.v
                pe.wait_ge(sDV.h, vkqt)
                a_gate = {}
                for c in range(16):
                    j = c // 2
                    is_up = c % 2
                    base = is_up * INNER + j * 512
                    ps = pcs[c % 2]
                    who = bank_free[c % 2]
                    if who:
                        pe.wait_ge(sDV.h, who)
                    for dt in range(DT):
                        mm_ = pe.matmul(ps[:], kqt[:, dt, :],
                                        wgu[:, dt, base:base + 512],
                                        start=(dt == 0), stop=(dt == DT - 1))
                    pv = sPE.inc(mm_)
                    dv.wait_ge(sPE.h, pv)
                    if not is_up:
                        if gv_free[j % 2]:
                            dv.wait_ge(sAC.h, gv_free[j % 2])
                        dvv = sDV.inc(dv.tensor_scalar(
                            gvb[:, j % 2, :], ps[:], rg, None, op0=OP.mult))
                        bank_free[c % 2] = dvv
                        ac.wait_ge(sDV.h, dvv)
                        if gs_free[j % 2]:
                            ac.wait_ge(sDV.h, gs_free[j % 2])
                        av = sAC.inc(ac.activation(out=gs[:, j % 2, :],
                                                   in_=gvb[:, j % 2, :],
                                                   func=AF.Silu))
                        gv_free[j % 2] = av
                        a_gate[j] = av
                    else:
                        dv.wait_ge(sAC.h, a_gate[j])
                        dvv = sDV.inc(dv.scalar_tensor_tensor(
                            hb[:, j * 512:(j + 1) * 512], ps[:], ru,
                            gs[:, j % 2, :], op0=OP.mult, op1=OP.mult))
                        bank_free[c % 2] = dvv
                        gs_free[j % 2] = dvv
                rf(dv.tensor_reduce(smt[:, 16:17], hb[:], AX, OP.max,
                                    apply_absolute_value=True))
                rf(dv.tensor_scalar(smt[:, 16:17], smt[:, 16:17], 1e-5, None,
                                    op0=OP.max))
                rf(dv.reciprocal(smt[:, 17:18], smt[:, 16:17]))
                rf(dv.tensor_scalar(smt[:, 17:18], smt[:, 17:18], 127.0, None,
                                    op0=OP.mult))
                dv.tensor_scalar(s2r_save[:, t:t + 1], smt[:, 16:17], 1.0 / 127,
                                 None, op0=OP.mult)
                dv.tensor_scalar(k2i[:], hb[:], smt[:, 17:18], None, op0=OP.mult)
                vk2 = sDV.inc(dv.tensor_copy(out=k2b[:], in_=k2i[:]))
                vk2_prev = vk2
                pe.wait_ge(sDV.h, vk2)
                if t >= 1:
                    dv.wait_ge(sGD.h, v_spill[t - 1])
                for blk in range(IT):
                    pool.run(k2t[:, blk * 128:(blk + 1) * 128],
                             k2b[:, blk * 128:(blk + 1) * 128])
                pe_tile_mark = sPE.v
                gp.wait_ge(sDV.h, sDV.v)
                v_spill[t] = sGD.inc(gp.dma_start(out=k2_d[t], in_=k2t[:]), 16)
            gp.wait_ge(sGD.h, sGD.v)
            nc.all_engine_barrier()

    def emit_ffn_b(x_src, dst, tag):
        with (
            nc.sbuf_tensor(f"{tag}_wdt", [128, IT, D], BF16) as wdt,
            nc.sbuf_tensor(f"{tag}_k2r", [128, 2, INNER], BF16) as k2r,
            nc.sbuf_tensor(f"{tag}_xb", [128, 2, D], F32) as xb,
            nc.sbuf_tensor(f"{tag}_ob", [128, 2, D], F32) as ob,
            nc.psum_tensor(f"{tag}_pd0", [128, 512], F32) as pd0,
            nc.psum_tensor(f"{tag}_pd1", [128, 512], F32) as pd1,
            nc.psum_tensor(f"{tag}_pd2", [128, 512], F32) as pd2,
            nc.psum_tensor(f"{tag}_pd3", [128, 512], F32) as pd3,
        ):
            pds = [[pd0, pd1], [pd2, pd3]]
            for it in range(IT):
                sBK.inc(sy.dma_start(out=wdt[:, it, :], in_=wd_t[it]), 16)
            v_wd = sBK.v
            pe.wait_ge(sBK.h, v_wd)
            v_k2 = [0] * (NT + 1)
            v_xa = [0] * (NT + 1)
            v_ev = [0] * NT
            v_st = [0] * NT
            pe_mark = [0, 0]
            v_k2[0] = sGP[0].inc(gp.dma_start(out=k2r[:, 0, :], in_=k2_d[0]), 16)
            v_xa[0] = sLP[0].inc(sy.dma_start(out=xb[:, 0, :], in_=x_src[0:128, :]), 16)
            for t in range(NT):
                if t + 1 < NT:
                    if t >= 1:
                        gp.wait_ge(sPE.h, pe_mark[(t + 1) % 2])
                        sy.wait_ge(sDV.h, v_ev[t - 1])
                    v_k2[t + 1] = sGP[(t + 1) % 2].inc(gp.dma_start(
                        out=k2r[:, (t + 1) % 2, :], in_=k2_d[t + 1]), 16)
                    v_xa[t + 1] = sLP[(t + 1) % 2].inc(sy.dma_start(
                        out=xb[:, (t + 1) % 2, :],
                        in_=x_src[(t + 1) * 128:(t + 2) * 128, :]), 16)
                pe.wait_ge(sGP[t % 2].h, v_k2[t])
                if t >= 2:
                    pe.wait_ge(sDV.h, v_ev[t - 2])
                pd = pds[t % 2]
                for it in range(IT):
                    for dc in range(2):
                        mm_ = pe.matmul(pd[dc][:],
                                        k2r[:, t % 2, it * 128:(it + 1) * 128],
                                        wdt[:, it, dc * 512:(dc + 1) * 512],
                                        start=(it == 0), stop=(it == IT - 1))
                pv = sPE.inc(mm_)
                pe_mark[t % 2] = pv
                dv.wait_ge(sPE.h, pv)
                dv.wait_ge(sLP[t % 2].h, v_xa[t])
                if t >= 2:
                    dv.wait_ge(sSP[t % 2].h, v_st[t - 2])
                dv.scalar_tensor_tensor(smt[:, 15:16], s2r_save[:, t:t + 1], 0.5,
                                        wdeq[:, 2:3], op0=OP.mult, op1=OP.mult)
                dv.drain()
                for dc in range(2):
                    ins_ = dv.scalar_tensor_tensor(
                        ob[:, t % 2, dc * 512:(dc + 1) * 512],
                        pd[dc][:], smt[:, 15:16],
                        xb[:, t % 2, dc * 512:(dc + 1) * 512],
                        op0=OP.mult, op1=OP.add)
                v_ev[t] = sDV.inc(ins_)
                sy.wait_ge(sDV.h, v_ev[t])
                v_st[t] = sSP[t % 2].inc(sy.dma_start(
                    out=dst[t * 128:(t + 1) * 128, :], in_=ob[:, t % 2, :]), 16)
            sy.wait_ge(sSP[0].h, sSP[0].v)
            sy.wait_ge(sSP[1].h, sSP[1].v)
            nc.all_engine_barrier()

    emit_ffn_a(x_in, "f1")
    wgu_ctx[0].__exit__(None, None, None)
    if PHASES <= 3:
        es.close()
        return nc
    emit_ffn_b(x_in, x1_d, "f1")
    if PHASES <= 4:
        es.close()
        return nc

    es3 = ExitStack()
    wfl = es3.enter_context(nc.sbuf_tensor("p3_wf", [128, 2, D], F32))
    whi = es3.enter_context(nc.sbuf_tensor("p3_hi", [128, 3, DT, D], BF16))
    wlo = es3.enter_context(nc.sbuf_tensor("p3_lo", [128, 3, DT, D], BF16))
    x1t = es3.enter_context(nc.sbuf_tensor("p3_x1", [128, 2, D], F32))
    sqb3 = es3.enter_context(nc.sbuf_tensor("p3_sq", [128, D], F32))
    lnb3 = es3.enter_context(nc.sbuf_tensor("p3_ln", [128, D], F32))
    lnh = es3.enter_context(nc.sbuf_tensor("p3_lnh", [128, D], BF16))
    lnl = es3.enter_context(nc.sbuf_tensor("p3_lnl", [128, D], BF16))
    xht = es3.enter_context(nc.sbuf_tensor("p3_xht", [128, DT, 512], BF16))
    xlt = es3.enter_context(nc.sbuf_tensor("p3_xlt", [128, DT, 512], BF16))
    ohb = es3.enter_context(nc.sbuf_tensor("p3_oh", [128, 2, 512], BF16))
    olb = es3.enter_context(nc.sbuf_tensor("p3_ol", [128, 2, 512], BF16))
    ps0 = es3.enter_context(nc.psum_tensor("p3_ps0", [128, 512], F32))
    ps1 = es3.enter_context(nc.psum_tensor("p3_ps1", [128, 512], F32))
    tq0 = es3.enter_context(nc.psum_tensor("p3_t0", [128, 128], BF16))
    tq1 = es3.enter_context(nc.psum_tensor("p3_t1", [128, 128], BF16))
    tq2 = es3.enter_context(nc.psum_tensor("p3_t2", [128, 128], BF16))
    tq3 = es3.enter_context(nc.psum_tensor("p3_t3", [128, 128], BF16))
    if True:
        pool3 = TPool([tq0, tq1, tq2, tq3])
        win = [wq_in, wk_in, wv_in]
        dvm = 0
        for w in range(3):
            for dint in range(DT):
                i = w * DT + dint
                if i >= 2:
                    sy.wait_ge(sDV.h, dvm)
                v = sLP[i % 2].inc(sy.dma_start(
                    out=wfl[:, i % 2, :],
                    in_=win[w][dint * 128:(dint + 1) * 128, :]), 16)
                dv.wait_ge(sLP[i % 2].h, v)
                dv.tensor_copy(out=whi[:, w, dint, :], in_=wfl[:, i % 2, :])
                dvm = sDV.inc(dv.tensor_tensor(
                    out=wlo[:, w, dint, :], in0=wfl[:, i % 2, :],
                    in1=whi[:, w, dint, :], op=OP.subtract))
        pe.wait_ge(sDV.h, dvm)

        psl = [ps0, ps1]
        bankf = [0, 0]
        obuf_free = [0, 0]
        pe_chunk_mark = 0
        x1_free = [0, 0]
        nstep = [0]

        def split_store(ps, slot, st_hi, st_lo):
            dv.wait_ge(sSP[slot].h, obuf_free[slot])
            dv.tensor_copy(out=ohb[:, slot, :], in_=ps[:])
            dvv = sDV.inc(dv.tensor_tensor(out=olb[:, slot, :], in0=ps[:],
                                           in1=ohb[:, slot, :], op=OP.subtract))
            sy.wait_ge(sDV.h, dvv)
            sSP[slot].inc(sy.dma_start(out=st_hi, in_=ohb[:, slot, :]), 16)
            vst = sSP[slot].inc(sy.dma_start(out=st_lo, in_=olb[:, slot, :]), 16)
            obuf_free[slot] = vst
            return dvv

        def proj_T(w, hi_d, lo_d, col0):
            for dout in range(DT):
                b = nstep[0] % 2
                ps = psl[b]
                pe.wait_ge(sDV.h, bankf[b])
                for dint in range(DT):
                    lh = whi[:, w, dint, dout * 128:(dout + 1) * 128]
                    ll = wlo[:, w, dint, dout * 128:(dout + 1) * 128]
                    pe.matmul(ps[:], lh, xht[:, dint, :], start=(dint == 0),
                              stop=False)
                    pe.matmul(ps[:], lh, xlt[:, dint, :], start=False, stop=False)
                    mm_ = pe.matmul(ps[:], ll, xht[:, dint, :], start=False,
                                    stop=(dint == DT - 1))
                pv = sPE.inc(mm_)
                dv.wait_ge(sPE.h, pv)
                bankf[b] = split_store(ps, b, hi_d[dout][:, col0:col0 + 512],
                                       lo_d[dout][:, col0:col0 + 512])
                nstep[0] += 1

        for ch in range(NCHUNK):
            for tt in range(4):
                t = ch * 4 + tt
                sy.wait_ge(sDV.h, x1_free[tt % 2])
                v = sLP[tt % 2].inc(sy.dma_start(
                    out=x1t[:, tt % 2, :],
                    in_=x1_d[t * 128:(t + 1) * 128, :]), 16)
                dv.wait_ge(sPE.h, pe_chunk_mark if tt == 0 else sPE.v)
                dv.wait_ge(sLP[tt % 2].h, v)
                emit_ln(x1t[:, tt % 2, :], lnb3[:], sqb3[:], sLP[tt % 2].h, v)
                dv.tensor_copy(out=lnh[:], in_=lnb3[:])
                vsp = sDV.inc(dv.tensor_tensor(out=lnl[:], in0=lnb3[:],
                                               in1=lnh[:], op=OP.subtract))
                x1_free[tt % 2] = vsp
                pe.wait_ge(sDV.h, vsp)
                for src, dst_t in ((lnh, xht), (lnl, xlt)):
                    for dt in range(DT):
                        pool3.run(dst_t[:, dt, tt * 128:(tt + 1) * 128],
                                  src[:, dt * 128:(dt + 1) * 128])
            vready = sDV.v
            pe.wait_ge(sDV.h, vready)

            proj_T(1, [cck_in[0, d] for d in range(DT)],
                   [cck_in[1, d] for d in range(DT)], ch * 512)
            for tt in range(4):
                t = ch * 4 + tt
                for dc in range(2):
                    b = nstep[0] % 2
                    ps = psl[b]
                    pe.wait_ge(sDV.h, bankf[b])
                    for dint in range(DT):
                        rh = whi[:, 2, dint, dc * 512:(dc + 1) * 512]
                        rl = wlo[:, 2, dint, dc * 512:(dc + 1) * 512]
                        xh_l = xht[:, dint, tt * 128:(tt + 1) * 128]
                        xl_l = xlt[:, dint, tt * 128:(tt + 1) * 128]
                        pe.matmul(ps[:], xh_l, rh, start=(dint == 0), stop=False)
                        pe.matmul(ps[:], xl_l, rh, start=False, stop=False)
                        mm_ = pe.matmul(ps[:], xh_l, rl, start=False,
                                        stop=(dint == DT - 1))
                    pv = sPE.inc(mm_)
                    dv.wait_ge(sPE.h, pv)
                    bankf[b] = split_store(
                        ps, b,
                        ccv_in[0, t, :, dc * 512:(dc + 1) * 512],
                        ccv_in[1, t, :, dc * 512:(dc + 1) * 512])
                    nstep[0] += 1
            proj_T(0, [qt_d[0, d] for d in range(DT)],
                   [qt_d[1, d] for d in range(DT)], ch * 512)
            pe_chunk_mark = sPE.v

        gp.wait_ge(sSP[0].h, sSP[0].v)
        gp.wait_ge(sSP[1].h, sSP[1].v)
        sCC.inc(gp.collective_compute(
            "AllGather", OP.bypass,
            replica_groups=[[0, 1], [2, 3], [4, 5], [6, 7]],
            ins=[cck_in[:]], outs=[cck_out[:]]))
        sCC.inc(gp.collective_compute(
            "AllGather", OP.bypass,
            replica_groups=[[0, 1], [2, 3], [4, 5], [6, 7]],
            ins=[ccv_in[:]], outs=[ccv_out[:]]))
        nc.all_engine_barrier()
    es3.close()
    if PHASES <= 5:
        es.close()
        return nc

    es4 = ExitStack()
    qh = es4.enter_context(nc.sbuf_tensor("p4_qh", [128, DT, 1024], BF16))
    ql = es4.enter_context(nc.sbuf_tensor("p4_ql", [128, DT, 1024], BF16))
    khb = es4.enter_context(nc.sbuf_tensor("p4_kh", [128, 2, DT, 512], BF16))
    klb = es4.enter_context(nc.sbuf_tensor("p4_kl", [128, 2, DT, 512], BF16))
    vb = es4.enter_context(nc.sbuf_tensor("p4_v", [128, 2, 4, 2, 1024], BF16))
    oacc = es4.enter_context(nc.sbuf_tensor("p4_oacc", [128, 8, D], F32))
    lacc = es4.enter_context(nc.sbuf_tensor("p4_lacc", [128, 8], F32))
    eb = es4.enter_context(nc.sbuf_tensor("p4_e", [128, 2, 512], F32))
    exb = es4.enter_context(nc.sbuf_tensor("p4_ex", [128, 2, 512], F32))
    phb = es4.enter_context(nc.sbuf_tensor("p4_ph", [128, 2, 512], BF16))
    plb = es4.enter_context(nc.sbuf_tensor("p4_pl", [128, 2, 512], BF16))
    x1r = es4.enter_context(nc.sbuf_tensor("p4_x1r", [128, 2, D], F32))
    x2s = es4.enter_context(nc.sbuf_tensor("p4_x2s", [128, 2, D], F32))
    rb = es4.enter_context(nc.sbuf_tensor("p4_r", [128, 8], F32))
    s_ps0 = es4.enter_context(nc.psum_tensor("p4_s0", [128, 512], F32))
    s_ps1 = es4.enter_context(nc.psum_tensor("p4_s1", [128, 512], F32))
    o_ps0 = es4.enter_context(nc.psum_tensor("p4_o0", [128, 512], F32))
    o_ps1 = es4.enter_context(nc.psum_tensor("p4_o1", [128, 512], F32))
    l_ps0 = es4.enter_context(nc.psum_tensor("p4_l0", [128, 512], F32))
    l_ps1 = es4.enter_context(nc.psum_tensor("p4_l1", [128, 512], F32))
    if True:
        sps = [s_ps0, s_ps1]
        ops_ = [o_ps0, o_ps1]
        lps = [l_ps0, l_ps1]
        vcb = sDV.inc(dv.memset(smt[:, 18:19], EXPC))
        ac.wait_ge(sDV.h, vcb)
        sy.wait_ge(sCC.h, sCC.v)

        slot_global = [0, 2, 1, 3]

        def load_ksuper(ks, buf):
            c0 = ks * 256
            for rank in range(2):
                for dt in range(DT):
                    sLP[buf].inc(sy.dma_start(
                        out=khb[:, buf, dt, rank * 256:rank * 256 + 256],
                        in_=cck_out[rank, 0, dt, :, c0:c0 + 256]), 16)
                    sLP[buf].inc(sy.dma_start(
                        out=klb[:, buf, dt, rank * 256:rank * 256 + 256],
                        in_=cck_out[rank, 1, dt, :, c0:c0 + 256]), 16)
                for lt in range(2):
                    slot = rank * 2 + lt
                    for hl in range(2):
                        sLP[buf].inc(sy.dma_start(
                            out=vb[:, buf, slot, hl, :],
                            in_=ccv_out[rank, hl, ks * 2 + lt]), 16)
            return sLP[buf].v

        kv_buf_free = [0, 0]
        sp_guard = [0, 0]
        eb_guard = [0, 0]
        exb_guard = [0, 0]
        p_guard = [0, 0]
        o_bank_free = [0, 0]
        l_bank_free = [0, 0]
        v_ks = {}
        v_ks[0] = load_ksuper(0, 0)
        pe_super_end = 0
        dv_super_end = 0
        for s in range(NSUP):
            if s > 0:
                sy.wait_ge(sPE.h, pe_super_end)
                sy.wait_ge(sDV.h, dv_super_end)
            for dt in range(DT):
                sBK.inc(sy.dma_start(out=qh[:, dt, :],
                                     in_=qt_d[0, dt, :, s * 1024:(s + 1) * 1024]), 16)
                sBK.inc(sy.dma_start(out=ql[:, dt, :],
                                     in_=qt_d[1, dt, :, s * 1024:(s + 1) * 1024]), 16)
            v_q = sBK.v
            pe.wait_ge(sBK.h, v_q)

            nks = 4 * s + 4
            for ks in range(nks):
                buf = ks % 2
                nxt = None
                if ks + 1 < nks:
                    nxt = ks + 1
                elif s + 1 < NSUP:
                    nxt = 0
                if nxt is not None:
                    sy.wait_ge(sPE.h, kv_buf_free[(ks + 1) % 2])
                    v_ks[nxt] = load_ksuper(nxt, (ks + 1) % 2)
                pe.wait_ge(sLP[buf].h, v_ks[ks])
                for cq in range(2):
                    nks_c = 4 * s + 2 * cq + 2
                    if ks >= nks_c:
                        continue
                    qcol = cq * 512
                    masked_super = ks >= nks_c - 2
                    ksrel = ks - (nks_c - 2)
                    for kt in range(4):
                        sp = sps[kt % 2]
                        if sp_guard[kt % 2]:
                            pe.wait_ge(sDV.h, sp_guard[kt % 2])
                        mask_list = []
                        if masked_super:
                            dlt = 4 * ksrel + slot_global[kt]
                            for x in range(4):
                                if dlt == 2 * x:
                                    mask_list.append((MU0, x))
                                elif dlt == 2 * x + 1:
                                    mask_list.append((MU1, x))
                                elif dlt >= 2 * x + 2:
                                    mask_list.append((MF, x))
                        for dint in range(DT):
                            lh = khb[:, buf, dint, kt * 128:(kt + 1) * 128]
                            ll = klb[:, buf, dint, kt * 128:(kt + 1) * 128]
                            last = (dint == DT - 1) and not mask_list
                            pe.matmul(sp[:], lh, qh[:, dint, qcol:qcol + 512],
                                      start=(dint == 0), stop=False)
                            pe.matmul(sp[:], lh, ql[:, dint, qcol:qcol + 512],
                                      start=False, stop=False)
                            mm_ = pe.matmul(sp[:], ll,
                                            qh[:, dint, qcol:qcol + 512],
                                            start=False, stop=last)
                        for mi_i, (mi, x) in enumerate(mask_list):
                            mm_ = pe.matmul(sp[:], masks[:, mi, :],
                                            iposn[:, x, :], start=False,
                                            stop=(mi_i == len(mask_list) - 1))
                        pv = sPE.inc(mm_)
                        dv.wait_ge(sPE.h, pv)
                        if eb_guard[kt % 2]:
                            dv.wait_ge(sAC.h, eb_guard[kt % 2])
                        dcp = sDV.inc(dv.tensor_copy(out=eb[:, kt % 2, :],
                                                     in_=sp[:]))
                        sp_guard[kt % 2] = dcp
                        ac.wait_ge(sDV.h, dcp)
                        if exb_guard[kt % 2]:
                            ac.wait_ge(sDV.h, exb_guard[kt % 2])
                        av = sAC.inc(ac.activation(
                            out=exb[:, kt % 2, :], in_=eb[:, kt % 2, :],
                            func=AF.Exp, bias=smt[:, 18:19], scale=1.0 / 32))
                        eb_guard[kt % 2] = av
                        dv.wait_ge(sAC.h, av)
                        if p_guard[kt % 2]:
                            dv.wait_ge(sPE.h, p_guard[kt % 2])
                        dv.tensor_copy(out=phb[:, kt % 2, :], in_=exb[:, kt % 2, :])
                        vspl = sDV.inc(dv.tensor_tensor(
                            out=plb[:, kt % 2, :], in0=exb[:, kt % 2, :],
                            in1=phb[:, kt % 2, :], op=OP.subtract))
                        exb_guard[kt % 2] = vspl
                        pe.wait_ge(sDV.h, vspl)
                        for qt in range(4):
                            phs = phb[:, kt % 2, qt * 128:(qt + 1) * 128]
                            pls = plb[:, kt % 2, qt * 128:(qt + 1) * 128]
                            for dc in range(2):
                                ob_ = ops_[(qt * 2 + dc) % 2]
                                if o_bank_free[(qt * 2 + dc) % 2]:
                                    pe.wait_ge(sDV.h,
                                               o_bank_free[(qt * 2 + dc) % 2])
                                vh = vb[:, buf, kt, 0, dc * 512:(dc + 1) * 512]
                                vl = vb[:, buf, kt, 1, dc * 512:(dc + 1) * 512]
                                pe.matmul(ob_[:], phs, vh, start=True, stop=False)
                                pe.matmul(ob_[:], phs, vl, start=False, stop=False)
                                pvo = sPE.inc(pe.matmul(ob_[:], pls, vh,
                                                        start=False, stop=True))
                                dv.wait_ge(sPE.h, pvo)
                                dst_sl = oacc[:, cq * 4 + qt,
                                              dc * 512:(dc + 1) * 512]
                                if ks == 0 and kt == 0:
                                    ins_ = dv.tensor_copy(out=dst_sl, in_=ob_[:])
                                else:
                                    ins_ = dv.tensor_add(dst_sl, dst_sl, ob_[:])
                                o_bank_free[(qt * 2 + dc) % 2] = sDV.inc(ins_)
                            lb_ = lps[qt % 2]
                            if l_bank_free[qt % 2]:
                                pe.wait_ge(sDV.h, l_bank_free[qt % 2])
                            pe.matmul(lb_[:], phs, ones512[:], start=True,
                                      stop=False)
                            pvl = sPE.inc(pe.matmul(lb_[:], pls, ones512[:],
                                                    start=False, stop=True))
                            dv.wait_ge(sPE.h, pvl)
                            lcol = lacc[:, cq * 4 + qt:cq * 4 + qt + 1]
                            if ks == 0 and kt == 0:
                                ins_ = dv.tensor_copy(out=lcol, in_=lb_[:, 0:1])
                            else:
                                ins_ = dv.tensor_add(lcol, lcol, lb_[:, 0:1])
                            l_bank_free[qt % 2] = sDV.inc(ins_)
                        p_guard[kt % 2] = sPE.v
                kv_buf_free[buf] = sPE.v
            pe_super_end = sPE.v
            dv.drain()
            rf(dv.reciprocal(rb[:], lacc[:]))
            stt_mark = [0, 0]
            st_mark = [0, 0]
            for q8 in range(8):
                t = s * 8 + q8
                if stt_mark[q8 % 2]:
                    sy.wait_ge(sDV.h, stt_mark[q8 % 2])
                vld = sBK.inc(sy.dma_start(out=x1r[:, q8 % 2, :],
                                           in_=x1_d[t * 128:(t + 1) * 128, :]), 16)
                dv.wait_ge(sBK.h, vld)
                if st_mark[q8 % 2]:
                    dv.wait_ge(sST.h, st_mark[q8 % 2])
                stt_mark[q8 % 2] = sDV.inc(dv.scalar_tensor_tensor(
                    x2s[:, q8 % 2, :], oacc[:, q8, :], rb[:, q8:q8 + 1],
                    x1r[:, q8 % 2, :], op0=OP.mult, op1=OP.add))
                sy.wait_ge(sDV.h, stt_mark[q8 % 2])
                st_mark[q8 % 2] = sST.inc(sy.dma_start(
                    out=x2_d[t * 128:(t + 1) * 128, :],
                    in_=x2s[:, q8 % 2, :]), 16)
            dv_super_end = stt_mark[1]
            sy.wait_ge(sST.h, sST.v)
        nc.all_engine_barrier()
    es4.close()
    if PHASES <= 6:
        es.close()
        return nc

    wgu_ctx[0] = nc.sbuf_tensor("st_wgu2", [128, DT, 2 * INNER], BF16)
    wgu_holder[0] = wgu_ctx[0].__enter__()
    for dt in range(DT):
        sBK.inc(sy.dma_start(out=wgu_holder[0][:, dt, :], in_=wgu_t[dt]), 16)
    v = sBK.v
    pe.wait_ge(sBK.h, v)
    dv.wait_ge(sBK.h, v)
    emit_ffn_a(x2_d, "f2")
    wgu_ctx[0].__exit__(None, None, None)
    emit_ffn_b(x2_d, out_d, "f2")

    if KDEBUG:
        dbg = {
            "dbg_wgu": (wgu_t, [DT, 128, 2 * INNER], BF16),
            "dbg_wd": (wd_t, [IT, 128, D], BF16),
            "dbg_x1": (x1_d, [TOK, D], F32),
            "dbg_x2": (x2_d, [TOK, D], F32),
            "dbg_qt": (qt_d, [2, DT, 128, TOK], BF16),
            "dbg_cck": (cck_in, [2, DT, 128, TOK], BF16),
            "dbg_ccv": (ccv_in, [2, NT, 128, D], BF16),
        }
        sy.wait_ge(sSP[0].h, sSP[0].v)
        sy.wait_ge(sSP[1].h, sSP[1].v)
        for nm, (src, shp, dt_) in dbg.items():
            o = nc.dram_tensor(nm, shp, dt_, kind="ExternalOutput").ap()
            sST.inc(sy.dma_start(out=o, in_=src), 16)
    sy.wait_ge(sST.h, sST.v)
    sy.wait_ge(sSP[0].h, sSP[0].v)
    sy.wait_ge(sSP[1].h, sSP[1].v)
    es.close()
    return nc


def kernel(L0_attn_x, L0_attn_Wq, L0_attn_Wk, L0_attn_Wv,
           L0_ffn_w_gate, L0_ffn_w_up, L0_ffn_w_down):
    x = np.ascontiguousarray(np.asarray(L0_attn_x, dtype=np.float32))
    B, S, D_ = x.shape

    tri = np.triu(np.full((128, 128), MASKVAL, np.float32), 1).astype(ml_dtypes.bfloat16)
    ful = np.full((128, 128), MASKVAL, np.float32).astype(ml_dtypes.bfloat16)
    zer = np.zeros((128, 128), ml_dtypes.bfloat16)
    ipos = np.zeros((128, 4, 512), ml_dtypes.bfloat16)
    for xq in range(4):
        ipos[:, xq, xq * 128:(xq + 1) * 128] = np.eye(128, dtype=ml_dtypes.bfloat16)

    consts = {
        "wq": np.ascontiguousarray(L0_attn_Wq, dtype=np.float32),
        "wk": np.ascontiguousarray(L0_attn_Wk, dtype=np.float32),
        "wv": np.ascontiguousarray(L0_attn_Wv, dtype=np.float32),
        "wg": np.ascontiguousarray(L0_ffn_w_gate, dtype=np.float32),
        "wu": np.ascontiguousarray(L0_ffn_w_up, dtype=np.float32),
        "wd": np.ascontiguousarray(L0_ffn_w_down, dtype=np.float32),
        "ident_bf": np.eye(128, dtype=ml_dtypes.bfloat16),
        "ones128": np.ones((128, 128), ml_dtypes.bfloat16),
        "ones512": np.ones((128, 512), ml_dtypes.bfloat16),
        "iposn": ipos,
    }

    xt = x.reshape(B, 64, 128, D_)
    in_maps = []
    for b in range(B):
        for p in range(2):
            tiles = list(range(p, 64, 2))
            m = dict(consts)
            m["x_loc"] = np.ascontiguousarray(xt[b, tiles].reshape(TOK, D_))
            if p == 0:
                m["mask0"], m["mask1"] = tri, ful
            else:
                m["mask0"], m["mask1"] = zer, tri
            m["mask2"] = ful
            in_maps.append(m)

    nc = build_program()
    res = run_bass_kernel_spmd(nc, in_maps, core_ids=list(range(8)))

    out = np.empty((B, 64, 128, D_), np.float32)
    for b in range(B):
        for p in range(2):
            tiles = list(range(p, 64, 2))
            out[b, tiles] = res.results[2 * b + p]["out_loc"].reshape(32, 128, D_)
    return out.reshape(B, S, D_)
